# version 06 (final)
# speedup vs baseline: 1.1721x; 1.0245x over previous
import math
import contextlib
from collections import defaultdict

import numpy as np

import concourse.bass as bass
import concourse.mybir as mybir
import bass_rust
from concourse.alu_op_type import AluOpType
from concourse.bass_utils import run_bass_kernel_spmd

F32 = mybir.dt.float32
BF16 = mybir.dt.bfloat16
I32 = mybir.dt.int32
AF = mybir.ActivationFunctionType
AX = mybir.AxisListType.X
OP = AluOpType

T = 2048
D = 1024
NT = 16
DC = 8
NE = 16
MID_R = 256
MID_S = 512
NSH = 2
IF = 3072
IFC = 24
NH = 16
HD = 64
SC = 256
NSC = T // SC

ALL_PARTS = ("l0f1", "l0mix", "l0f2", "l1f1", "l1att", "l1f2")
ENGINES = ("sync", "tensor", "vector", "scalar", "gpsimd")
DBG = False
DEQ_JOBS = ("wu", "wg", "wd")
QDQ_MAX = 9
LAST_DBG_INDEX = []


def dbg_tap(p, g, ap2d, tag):
    if not DBG:
        return
    key = "dbgf" if ap2d.dtype == F32 else "dbgb"
    off = g[key + "_off"]
    n = ap2d.shape[-1] if len(ap2d.shape) == 2 else int(np.prod(ap2d.shape[1:]))
    g.setdefault("dbg_index", []).append((tag, key, off, n))
    p.wait("gpsimd", "dve")
    p.wait("gpsimd", "act")
    p.prod("gpsimd", "st", lambda gp, ap2d=ap2d, off=off, n=n, key=key:
           gp.dma_start(out=g[key][:, off:off + n], in_=ap2d), 16)
    g[key + "_off"] = off + n


def _slopes(n):
    start = 2.0 ** (-2.0 ** (-(math.log2(n) - 3)))
    return [start * start ** i for i in range(n)]


class P:
    def __init__(self, nc):
        self.nc = nc
        self.ops = []
        self.c = defaultdict(int)
        self.sems = {}

    def prod(self, eng, name, f, n=1):
        sem = self.sems[name]

        def gfn(e, f=f, sem=sem, n=n):
            f(e).then_inc(sem, n)
        self.ops.append((eng, gfn))
        self.c[name] += n
        return self.c[name]

    def wait(self, eng, name, val=None):
        sem = self.sems[name]
        v = self.c[name] if val is None else val
        self.ops.append((eng, lambda e, sem=sem, v=v: e.wait_ge(sem, v)))

    def raw(self, eng, f):
        self.ops.append((eng, f))

    def flush(self):
        ops = self.ops
        self.ops = []
        with self.nc.Block() as blk:
            for name in ENGINES:
                fns = [f for (e, f) in ops if e == name]
                if not fns:
                    continue

                def body(eng, fns=fns):
                    for f in fns:
                        f(eng)
                getattr(blk, name)(body)




def psum_list(nc, stack, prefix, n, shape, dt=F32):
    return [stack.enter_context(nc.psum_tensor(f"{prefix}{i}", shape, dt))
            for i in range(n)]


def load_weight_bf16(p, g, dram_ap, dst3, stage):
    src = dram_ap.rearrange("(k p) e -> p k e", p=128)
    pw = stage.shape[1] // D
    st3 = stage.rearrange("p (a e) -> p a e", a=pw)
    for i in range(DC // pw):
        p.wait("sync", "dve")
        p.prod("sync", "ld", lambda s, i=i: s.dma_start(
            out=st3, in_=src[:, pw * i:pw * (i + 1), :]), 16)
        p.wait("vector", "ld")
        p.prod("vector", "dve", lambda v, i=i: v.tensor_copy(
            dst3[:, pw * i:pw * (i + 1), :], st3))


def ln_pass(p, g, x_res, tiles, dst, dst_off):
    stats, mv, sa = g["ln_stats"], g["ln_mv"], g["ln_sa"]
    for i in tiles:
        xt = x_res[:, i * D:(i + 1) * D]
        def stat_chain(v, xt=xt):
            v.bn_stats(stats[:, 0:6], xt[:, 0:512])
            v.bn_stats(stats[:, 6:12], xt[:, 512:1024])
            v.bn_aggr(mv[:, 0:2], stats[:, 0:6])
            v.bn_aggr(mv[:, 2:4], stats[:, 6:12])
            v.drain()
            v.tensor_tensor(sa[:, 0:1], mv[:, 0:1], mv[:, 2:3], op=OP.add)
            v.tensor_tensor(sa[:, 1:2], mv[:, 0:1], mv[:, 0:1], op=OP.mult)
            v.tensor_tensor(sa[:, 2:3], mv[:, 2:3], mv[:, 2:3], op=OP.mult)
            v.drain()
            v.tensor_scalar(mv[:, 4:5], sa[:, 0:1], 0.5, None, op0=OP.mult)
            v.tensor_tensor(sa[:, 1:2], sa[:, 1:2], mv[:, 1:2], op=OP.add)
            v.tensor_tensor(sa[:, 2:3], sa[:, 2:3], mv[:, 3:4], op=OP.add)
            v.drain()
            v.tensor_tensor(sa[:, 1:2], sa[:, 1:2], sa[:, 2:3], op=OP.add)
            v.tensor_tensor(sa[:, 2:3], mv[:, 4:5], mv[:, 4:5], op=OP.mult)
            v.drain()
            v.tensor_scalar(sa[:, 1:2], sa[:, 1:2], 0.5, None, op0=OP.mult)
            v.drain()
            v.tensor_tensor(mv[:, 5:6], sa[:, 1:2], sa[:, 2:3], op=OP.subtract)
            v.drain()
            return v.tensor_scalar(mv[:, 6:7], mv[:, 5:6], 1e-5, None, op0=OP.add)
        p.prod("vector", "dve", stat_chain)
        p.wait("scalar", "dve")
        p.prod("scalar", "act",
               lambda s: s.activation(mv[:, 7:8], mv[:, 6:7], AF.Sqrt))
        p.wait("vector", "act")
        dcol = dst[:, (i - dst_off) * D:(i - dst_off + 1) * D]
        def norm_chain(v, dcol=dcol, xt=xt):
            v.reciprocal(sa[:, 0:1], mv[:, 7:8])
            v.drain()
            v.tensor_tensor(sa[:, 1:2], sa[:, 0:1], sa[:, 0:1], op=OP.mult)
            v.drain()
            v.tensor_tensor(sa[:, 2:3], mv[:, 6:7], sa[:, 1:2], op=OP.mult)
            v.drain()
            v.tensor_scalar(sa[:, 3:4], sa[:, 2:3], -0.5, 1.5, op0=OP.mult,
                            op1=OP.add)
            v.drain()
            v.tensor_tensor(sa[:, 4:5], sa[:, 0:1], sa[:, 3:4], op=OP.mult)
            v.drain()
            v.scalar_tensor_tensor(sa[:, 5:6], mv[:, 4:5], -1.0, sa[:, 4:5],
                                   op0=OP.mult, op1=OP.mult)
            v.drain()
            return v.tensor_scalar(dcol, xt, sa[:, 4:5], sa[:, 5:6], op0=OP.mult,
                                   op1=OP.add)
        p.prod("vector", "dve", norm_chain)


def transpose_pass(p, g, items, dtype):
    trp = g["trp_bf"] if dtype == BF16 else g["trp_f32"]
    ident = g["ident_bf"] if dtype == BF16 else g["ident_f32"]
    p.wait("tensor", "dve")
    pe0 = p.c["pe"]
    ev0 = p.c["dve"]
    for j, (src, dst) in enumerate(items):
        if j >= 2:
            p.wait("tensor", "dve", ev0 + j - 1)
        p.prod("tensor", "pe",
               lambda t, src=src, j=j: t.transpose(trp[j % 2][:], src, ident[:]))
        p.wait("vector", "pe", pe0 + j + 1)
        p.prod("vector", "dve",
               lambda v, dst=dst, j=j: v.tensor_copy(dst, trp[j % 2][:]))


def global_max_cols(p, nc, g, src_cols, n, dst_128xn):
    cc_in, cc_out, cc_sb = g["cc_in"], g["cc_out"], g["cc_sb"]
    cs2 = g["cc_sb2"]
    arp = g["ar_psum"]
    p.wait("tensor", "dve")
    p.prod("tensor", "pe", lambda t: t.transpose(
        arp[0:n, 0:128], src_cols, g["ident_f32"][:]))
    p.wait("vector", "pe")
    p.prod("vector", "dve", lambda v: v.tensor_reduce(
        cs2[0:n, 0:1], arp[0:n, 0:128], axis=AX, op=OP.max))
    p.wait("gpsimd", "dve")
    p.prod("gpsimd", "st", lambda gp: gp.dma_start(out=cc_in[0:n, 0:1],
                                                   in_=cs2[0:n, 0:1]), 16)
    p.wait("gpsimd", "st")
    p.prod("gpsimd", "cc", lambda gp: gp.collective_compute(
        "AllReduce", OP.max, replica_groups=[list(range(8))],
        ins=[cc_in.opt()], outs=[cc_out.opt()]))
    p.wait("gpsimd", "cc")
    p.prod("gpsimd", "ld", lambda gp: gp.dma_start(out=cc_sb[0:1, 4:8],
                                                   in_=cc_out[:]), 16)
    p.wait("tensor", "ld")
    p.wait("tensor", "dve")
    p.prod("tensor", "pe", lambda t: t.matmul(
        arp[0:128, 0:n], g["ones_f32"][0:1, 0:128], cc_sb[0:1, 4:4 + n],
        start=True, stop=True))
    p.wait("vector", "pe")
    p.prod("vector", "dve", lambda v: (v.tensor_copy(dst_128xn, arp[0:128, 0:n]),
                                       v.drain())[0])



def emit_moe(p, nc, g, tag):
    x_res = g["x_res"]
    for q in range(4):
        tiles = [4 * q + i for i in range(4)]
        with (
            nc.sbuf_tensor(f"m_xb{tag}{q}", [128, DC * 512], BF16) as xb,
            nc.sbuf_tensor(f"m_cb{tag}{q}", [128, 4 * NE], F32) as comb,
            nc.sbuf_tensor(f"m_H{tag}{q}", [128, 40 * 512], BF16) as H,
        ):
            xbv = xb[:].rearrange("p (k t) -> p k t", k=DC)
            Hv = H[:].rearrange("p (k t) -> p k t", k=40)

            with (
                nc.sbuf_tensor(f"m_xn{tag}{q}", [128, 4 * D], F32) as xn,
                nc.sbuf_tensor(f"m_xf{tag}{q}", [128, DC * 512], F32) as xf,
                nc.sbuf_tensor(f"m_sm{tag}{q}", [128, 8 * NE + 2], F32) as sm,
                nc.psum_tensor(f"m_lg{tag}{q}", [128, NE], F32) as lg,
                nc.psum_tensor(f"m_tf0{tag}{q}", [128, 128], F32) as tf0,
                nc.psum_tensor(f"m_tf1{tag}{q}", [128, 128], F32) as tf1,
            ):
                g["trp_f32"] = (tf0, tf1)
                xfv = xf[:].rearrange("p (k t) -> p k t", k=DC)
                if q == 0:
                    dbg_tap(p, g, x_res[:, 0:1024], f"xres_{tag}")
                ln_pass(p, g, x_res, tiles, xn, 4 * q)
                if q == 0:
                    dbg_tap(p, g, g["ln_mv"][:, 0:8], f"mv_{tag}")
                    dbg_tap(p, g, g["ln_stats"][:, 0:12], f"stats_{tag}")
                items = []
                for ii in range(4):
                    for c in range(DC):
                        items.append((xn[:, ii * D + 128 * c:ii * D + 128 * c + 128],
                                      xfv[:, c, 128 * ii:128 * (ii + 1)]))
                transpose_pass(p, g, items, F32)
                p.wait("vector", "pe")
                p.prod("vector", "dve", lambda v: v.tensor_copy(xb[:], xf[:]))
                if q == 0:
                    dbg_tap(p, g, xf[:, 0:512], f"xf0_{tag}")
                    dbg_tap(p, g, xn[:, 0:1024], f"xn0_{tag}")
                rw = g["rw_f32"]
                sa = g["ln_sa"]
                L, E = sm[:, 0:NE], sm[:, NE:2 * NE]
                nmx = sm[:, 8 * NE:8 * NE + 1]
                for ii in range(4):
                    p.wait("tensor", "dve")
                    if ii > 0:
                        p.wait("tensor", "aux")

                    def rmm(t, ii=ii):
                        for k in range(DC):
                            ins = t.matmul(lg[:],
                                           xfv[:, k, 128 * ii:128 * (ii + 1)],
                                           rw[:, k * NE:(k + 1) * NE],
                                           start=(k == 0), stop=(k == DC - 1))
                        return ins
                    p.prod("tensor", "pe", rmm)
                    p.wait("vector", "pe")
                    def rpre(v):
                        v.tensor_copy(L, lg[:])
                        v.drain()
                        v.tensor_reduce(sa[:, 0:1], L, axis=AX, op=OP.max)
                        v.drain()
                        return v.tensor_scalar(nmx, sa[:, 0:1], -1.0, None,
                                               op0=OP.mult)
                    p.prod("vector", "dve", rpre)
                    p.wait("scalar", "dve")
                    p.prod("scalar", "act",
                           lambda s: s.activation(E, L, AF.Exp, bias=nmx))
                    p.wait("vector", "act")

                    def top2(v, ii=ii):
                        P_, eq1, inv, P2, eq2, c1 = (sm[:, 2 * NE:3 * NE],
                                                     sm[:, 3 * NE:4 * NE],
                                                     sm[:, 4 * NE:5 * NE],
                                                     sm[:, 5 * NE:6 * NE],
                                                     sm[:, 6 * NE:7 * NE],
                                                     sm[:, 7 * NE:8 * NE])
                        v.tensor_reduce(sa[:, 0:1], E, axis=AX, op=OP.add)
                        v.drain()
                        v.reciprocal(sa[:, 1:2], sa[:, 0:1])
                        v.drain()
                        v.tensor_scalar(P_, E, sa[:, 1:2], None, op0=OP.mult)
                        v.drain()
                        v.tensor_reduce(sa[:, 2:3], P_, axis=AX, op=OP.max)
                        v.drain()
                        v.tensor_scalar(eq1, P_, sa[:, 2:3], None, op0=OP.is_ge)
                        v.drain()
                        v.tensor_scalar(inv, eq1, -1.0, 1.0, op0=OP.mult,
                                        op1=OP.add)
                        v.drain()
                        v.tensor_tensor(P2, P_, inv, op=OP.mult)
                        v.drain()
                        v.tensor_reduce(sa[:, 3:4], P2, axis=AX, op=OP.max)
                        v.drain()
                        v.tensor_scalar(eq2, P2, sa[:, 3:4], None, op0=OP.is_ge)
                        v.tensor_tensor(sa[:, 4:5], sa[:, 2:3], sa[:, 3:4],
                                        op=OP.add)
                        v.drain()
                        v.tensor_scalar(sa[:, 4:5], sa[:, 4:5], 1e-9, None,
                                        op0=OP.add)
                        v.drain()
                        v.reciprocal(sa[:, 5:6], sa[:, 4:5])
                        v.drain()
                        v.tensor_tensor(sa[:, 6:7], sa[:, 2:3], sa[:, 5:6],
                                        op=OP.mult)
                        v.tensor_tensor(sa[:, 7:8], sa[:, 3:4], sa[:, 5:6],
                                        op=OP.mult)
                        v.drain()
                        v.tensor_scalar(c1, eq1, sa[:, 6:7], None, op0=OP.mult)
                        v.drain()
                        return v.scalar_tensor_tensor(
                            comb[:, ii * NE:(ii + 1) * NE], eq2, sa[:, 7:8], c1,
                            op0=OP.mult, op1=OP.add)
                    p.prod("vector", "aux", top2)
                if q == 0:
                    p.wait("gpsimd", "aux")
                    dbg_tap(p, g, comb[:, 0:64], f"comb_{tag}")
                p.flush()

            with (
                nc.sbuf_tensor(f"m_ws{tag}{q}", [128, 2 * DC * MID_S], BF16) as ws,
                nc.sbuf_tensor(f"m_sg{tag}{q}", [128, 512], F32) as sg,
                nc.sbuf_tensor(f"m_we{tag}{q}", [128, 2 * 2 * DC * MID_R],
                               BF16) as we,
                nc.sbuf_tensor(f"m_he{tag}{q}", [128, 2 * MID_R], F32) as he,
                nc.sbuf_tensor(f"m_hb{tag}{q}", [128, 2 * MID_R], BF16) as hb,
                nc.psum_tensor(f"m_pg{tag}{q}", [128, 512], F32) as pg,
                nc.psum_tensor(f"m_pu{tag}{q}", [128, 512], F32) as pu,
                nc.psum_tensor(f"m_pg2{tag}{q}", [128, 512], F32) as pg2,
                nc.psum_tensor(f"m_pu2{tag}{q}", [128, 512], F32) as pu2,
                nc.psum_tensor(f"m_tp0{tag}{q}", [128, 128], BF16) as tp0,
                nc.psum_tensor(f"m_tp1{tag}{q}", [128, 128], BF16) as tp1,
            ):
                wsv = ws[:].rearrange("p (a k m) -> p a k m", a=2, k=DC)
                for n in range(NSH):
                    if n > 0:
                        p.wait("sync", "pe")
                    p.prod("sync", "ld", lambda s, n=n: s.dma_start(
                        out=wsv[:, 0],
                        in_=g["s_wgs"][n].rearrange("(k p) m -> p k m", p=128)),
                        16)
                    p.prod("sync", "ld", lambda s, n=n: s.dma_start(
                        out=wsv[:, 1],
                        in_=g["s_wus"][n].rearrange("(k p) m -> p k m", p=128)),
                        16)
                    p.wait("tensor", "ld")
                    for m in range(4):
                        def smm(t, n=n, m=m):
                            for k in range(DC):
                                t.matmul(pg[:], wsv[:, 0, k, 128 * m:128 * (m + 1)],
                                         xbv[:, k, :], start=(k == 0),
                                         stop=(k == DC - 1))
                            for k in range(DC):
                                ins = t.matmul(
                                    pu[:], wsv[:, 1, k, 128 * m:128 * (m + 1)],
                                    xbv[:, k, :], start=(k == 0),
                                    stop=(k == DC - 1))
                            return ins
                        p.prod("tensor", "pe", smm)
                        p.wait("scalar", "pe")
                        p.prod("scalar", "act",
                               lambda s: s.activation(sg[:], pg[:], AF.Silu))
                        p.wait("vector", "act")
                        p.prod("vector", "dve", lambda v, n=n, m=m: v.tensor_tensor(
                            Hv[:, 32 + 4 * n + m, :], sg[:], pu[:], op=OP.mult))
                        p.wait("tensor", "dve")

                wev = we[:].rearrange("p (s a k m) -> p s a k m", s=2, a=2, k=DC)
                pe_done = {}
                epi_dve = {}
                tr_pe = {}
                pgs = [pg, pg2]
                pus = [pu, pu2]
                hbs = [hb[:, 0:MID_R], hb[:, MID_R:2 * MID_R]]
                sgs = [sg[:, 0:MID_R], sg[:, MID_R:2 * MID_R]]
                hes = [he[:, 0:MID_R], he[:, MID_R:2 * MID_R]]

                def emit_tr(it):
                    e1, ii1 = divmod(it, 4)
                    par1 = it % 2
                    p.wait("tensor", "dve", epi_dve[it])
                    p.prod("tensor", "pe", lambda t, par1=par1: t.transpose(
                        tp0[:], hbs[par1][:, 0:128], g["ident_bf"][:]))
                    p.prod("tensor", "pe", lambda t, par1=par1: t.transpose(
                        tp1[:], hbs[par1][:, 128:256], g["ident_bf"][:]))
                    tr_pe[it] = p.c["pe"]
                    p.wait("vector", "pe", tr_pe[it])
                    p.prod("vector", "dve", lambda v, e1=e1, ii1=ii1: (
                        v.tensor_copy(Hv[:, 2 * e1, 128 * ii1:128 * (ii1 + 1)],
                                      tp0[:]),
                        v.tensor_copy(Hv[:, 2 * e1 + 1,
                                         128 * ii1:128 * (ii1 + 1)],
                                      tp1[:]))[-1])

                for e in range(NE):
                    sl = e % 2
                    if e >= 2:
                        p.wait("sync", "pe", pe_done[e - 2])
                    p.prod("sync", "ld", lambda s, e=e, sl=sl: s.dma_start(
                        out=wev[:, sl, 0],
                        in_=g["s_wgr"][e].rearrange("(k p) m -> p k m", p=128)),
                        16)
                    p.prod("sync", "ld", lambda s, e=e, sl=sl: s.dma_start(
                        out=wev[:, sl, 1],
                        in_=g["s_wur"][e].rearrange("(k p) m -> p k m", p=128)),
                        16)
                    ld_e = p.c["ld"]
                    wek = wev[:, sl, 0]
                    wuk = wev[:, sl, 1]
                    for ii in range(4):
                        it = 4 * e + ii
                        par = it % 2
                        if ii == 0:
                            p.raw("tensor", lambda t, ld_e=ld_e: t.wait_ge(
                                p.sems["ld"], ld_e))
                        if it >= 2:
                            p.raw("tensor", lambda t, v=epi_dve[it - 2]:
                                  t.wait_ge(p.sems["dve"], v))

                        def emm(t, ii=ii, wek=wek, wuk=wuk, par=par):
                            for k in range(DC):
                                t.matmul(pgs[par][:, 0:MID_R],
                                         xbv[:, k, 128 * ii:128 * (ii + 1)],
                                         wek[:, k], start=(k == 0),
                                         stop=(k == DC - 1))
                            for k in range(DC):
                                ins = t.matmul(pus[par][:, 0:MID_R],
                                               xbv[:, k, 128 * ii:128 * (ii + 1)],
                                               wuk[:, k], start=(k == 0),
                                               stop=(k == DC - 1))
                            return ins
                        p.prod("tensor", "pe", emm)
                        mm_pe = p.c["pe"]
                        p.wait("scalar", "pe", mm_pe)
                        p.prod("scalar", "act", lambda s, par=par: s.activation(
                            sgs[par], pgs[par][:, 0:MID_R], AF.Silu))
                        act_it = p.c["act"]
                        cs = comb[:, ii * NE + e:ii * NE + e + 1]
                        p.wait("vector", "act", act_it)
                        p.prod("vector", "dve", lambda v, cs=cs, par=par: (
                            v.tensor_tensor(hes[par], sgs[par],
                                            pus[par][:, 0:MID_R], op=OP.mult),
                            v.tensor_scalar(hbs[par], hes[par], cs, None,
                                            op0=OP.mult))[-1])
                        epi_dve[it] = p.c["dve"]
                        if it >= 1:
                            emit_tr(it - 1)
                    pe_done[e] = p.c["pe"]
                emit_tr(4 * NE - 1)
                if q == 0:
                    p.wait("gpsimd", "dve")
                    dbg_tap(p, g, H[:, 0:512], f"H_e0a_{tag}")
                    dbg_tap(p, g, H[:, 32 * 512:32 * 512 + 512], f"H_s0_{tag}")
                p.flush()

            with nc.sbuf_tensor(f"m_wd{tag}{q}", [128, 20 * D], BF16) as wd:
                ps_stack = contextlib.ExitStack()
                pds = psum_list(nc, ps_stack, f"m_pd{tag}{q}_", 8, [128, 512])
                wdv = wd[:].rearrange("p (k d) -> p k d", k=20)
                for half in range(2):
                    if half == 1:
                        p.wait("sync", "pe")
                    for kk in range(20):
                        kg = 20 * half + kk
                        if kg < 32:
                            e, j = divmod(kg, 2)
                            src = g["s_wdr"][e].rearrange("(j p) d -> j p d",
                                                          p=128)[j]
                        else:
                            nsh, j = divmod(kg - 32, 4)
                            src = g["s_wds"][nsh].rearrange("(j p) d -> j p d",
                                                            p=128)[j]
                        p.prod("sync", "ld", lambda s, src=src, kk=kk: s.dma_start(
                            out=wdv[:, kk], in_=src), 16)
                    p.wait("tensor", "ld")

                    def dmm(t, half=half):
                        for ii in range(4):
                            for n2 in range(2):
                                for kk in range(20):
                                    kg = 20 * half + kk
                                    ins = t.matmul(
                                        pds[2 * ii + n2][:],
                                        Hv[:, kg, 128 * ii:128 * (ii + 1)],
                                        wdv[:, kk, 512 * n2:512 * (n2 + 1)],
                                        start=(kg == 0), stop=(kg == 39))
                        return ins
                    p.prod("tensor", "pe", dmm)
                p.wait("vector", "pe")
                for ii in range(4):
                    for n2 in range(2):
                        i = 4 * q + ii
                        xs = x_res[:, i * D + 512 * n2:i * D + 512 * (n2 + 1)]
                        p.prod("vector", "dve", lambda v, xs=xs, ii=ii, n2=n2:
                               v.scalar_tensor_tensor(xs, pds[2 * ii + n2][:], 0.5,
                                                      xs, op0=OP.mult, op1=OP.add))
                p.flush()
                ps_stack.close()



def emit_mixer(p, nc, g):
    x_res = g["x_res"]
    with (
        nc.sbuf_tensor("mx_xm", [128, DC * T], BF16) as xm,
        nc.sbuf_tensor("mx_inp", [128, DC * T], BF16) as inp,
        nc.sbuf_tensor("mx_w", [128, 2 * DC * D], BF16) as wbuf,
        nc.sbuf_tensor("mx_wf", [128, 2 * D], F32) as wfbuf,
        nc.sbuf_tensor("mx_ev", [128, 512], F32) as evf,
        nc.sbuf_tensor("mx_ev2", [128, 512], F32) as evf2,
        nc.sbuf_tensor("mx_ev3", [128, 512], BF16) as evf3,
    ):
        xmv = xm[:].rearrange("p (k t) -> p k t", k=DC)
        inpv = inp[:].rearrange("p (k t) -> p k t", k=DC)

        with (
            nc.sbuf_tensor("mx_xn", [128, 4 * D], BF16) as xn,
            nc.psum_tensor("mx_tb0", [128, 128], BF16) as tb0,
            nc.psum_tensor("mx_tb1", [128, 128], BF16) as tb1,
        ):
            g["trp_bf"] = (tb0, tb1)
            for q in range(4):
                tiles = [4 * q + i for i in range(4)]
                ln_pass(p, g, x_res, tiles, xn, 4 * q)
                items = []
                for ii in range(4):
                    for c in range(DC):
                        items.append(
                            (xn[:, ii * D + 128 * c:ii * D + 128 * c + 128],
                             xmv[:, c, 512 * q + 128 * ii:
                                 512 * q + 128 * ii + 128]))
                transpose_pass(p, g, items, BF16)
                p.flush()

        wbv = wbuf[:].rearrange("p (c k e) -> p c k e", c=2, k=DC)

        def load_w(dram_t, col):
            load_weight_bf16(p, g, dram_t, wbv[:, col], wfbuf[:])

        with (
            nc.psum_tensor("mx_pp0", [128, 512], F32) as pp0,
            nc.psum_tensor("mx_pp1", [128, 512], F32) as pp1,
        ):
            pps = [pp0, pp1]
            load_w(g["l0Win_T"], 0)
            p.wait("tensor", "dve")
            wk = wbv[:, 0]
            act0 = p.c["act"]
            jj = 0
            for tq in range(4):
                for m in range(DC):
                    if jj >= 2:
                        p.wait("tensor", "act", act0 + jj - 1)

                    def pmm(t, tq=tq, m=m, jj=jj):
                        for k in range(DC):
                            ins = t.matmul(pps[jj % 2][:],
                                           wk[:, k, 128 * m:128 * (m + 1)],
                                           xmv[:, k, 512 * tq:512 * (tq + 1)],
                                           start=(k == 0), stop=(k == DC - 1))
                        return ins
                    p.prod("tensor", "pe", pmm)
                    p.wait("scalar", "pe")
                    bia = g["bias_cm"][:, 0 * 8 + m:0 * 8 + m + 1]
                    p.prod("scalar", "act", lambda s, tq=tq, m=m, jj=jj, bia=bia:
                           s.activation(inpv[:, m, 512 * tq:512 * (tq + 1)],
                                        pps[jj % 2][:], AF.Identity, bias=bia))
                    jj += 1
            p.flush()

            load_w(g["l0Wg_T"], 0)
            load_w(g["l0B_T"], 1)
            p.wait("tensor", "dve")
            wgk = wbv[:, 0]
            wuk = wbv[:, 1]
            act1 = p.c["act"]
            jj = 0
            for tq in range(4):
                for m in range(DC):
                    if jj >= 2:
                        p.wait("tensor", "act", act1 + jj - 1)

                    def gmm(t, tq=tq, m=m, jj=jj):
                        for k in range(DC):
                            ins = t.matmul(pps[jj % 2][:],
                                           wgk[:, k, 128 * m:128 * (m + 1)],
                                           inpv[:, k, 512 * tq:512 * (tq + 1)],
                                           start=(k == 0), stop=(k == DC - 1))
                        return ins
                    p.prod("tensor", "pe", gmm)
                    p.wait("scalar", "pe")
                    p.wait("scalar", "st")
                    bia = g["bias_cm"][:, 1 * 8 + m:1 * 8 + m + 1]
                    p.prod("scalar", "act", lambda s, bia=bia, jj=jj:
                           s.activation(evf[:], pps[jj % 2][:], AF.Sigmoid,
                                        bias=bia))
                    p.wait("vector", "act")
                    ih = inpv[:, m, 512 * tq:512 * (tq + 1)]
                    p.prod("vector", "dve", lambda v, ih=ih: (
                        v.tensor_tensor(evf2[:], evf[:], ih, op=OP.mult),
                        v.scalar_tensor_tensor(evf2[:], evf2[:], -1.0, ih,
                                               op0=OP.mult, op1=OP.add))[-1])
                    p.wait("gpsimd", "dve")
                    for h2 in range(2):
                        scid = 2 * tq + h2
                        p.prod("gpsimd", "st", lambda gp, scid=scid, m=m, h2=h2:
                               gp.dma_start(
                                   out=g["s_g"][scid, :, m * SC:(m + 1) * SC],
                                   in_=evf[:, 256 * h2:256 * (h2 + 1)]), 16)
                        p.prod("gpsimd", "st", lambda gp, scid=scid, m=m, h2=h2:
                               gp.dma_start(
                                   out=g["s_i2"][scid, :, m * SC:(m + 1) * SC],
                                   in_=evf2[:, 256 * h2:256 * (h2 + 1)]), 16)
                    jj += 1
                    if jj >= 2:
                        p.wait("tensor", "act", act1 + jj - 1)

                    def umm(t, tq=tq, m=m, jj=jj):
                        for k in range(DC):
                            ins = t.matmul(pps[jj % 2][:],
                                           wuk[:, k, 128 * m:128 * (m + 1)],
                                           inpv[:, k, 512 * tq:512 * (tq + 1)],
                                           start=(k == 0), stop=(k == DC - 1))
                        return ins
                    p.prod("tensor", "pe", umm)
                    p.wait("scalar", "pe")
                    p.wait("scalar", "st")
                    p.prod("scalar", "act", lambda s, jj=jj:
                           s.activation(evf3[:], pps[jj % 2][:], AF.Copy))
                    p.wait("gpsimd", "act")
                    for h2 in range(2):
                        scid = 2 * tq + h2
                        p.prod("gpsimd", "st", lambda gp, scid=scid, m=m, h2=h2:
                               gp.dma_start(
                                   out=g["s_u"][scid, :, m * SC:(m + 1) * SC],
                                   in_=evf3[:, 256 * h2:256 * (h2 + 1)]), 16)
                    jj += 1
            p.flush()

    if DBG:
        dbg_tap(p, g, xm[:, 0:512], "xm0")
        dbg_tap(p, g, inp[:, 0:512], "inp0")
        for nm in ("s_g", "s_i2"):
            off = g["dbgf_off"]
            g.setdefault("dbg_index", []).append((nm + "0", "dbgf", off, 2048))
            p.prod("gpsimd", "st", lambda gp, nm=nm, off=off: gp.dma_start(
                out=g["dbgf"][:, off:off + 2048], in_=g[nm][0]), 16)
            g["dbgf_off"] = off + 2048
        off = g["dbgb_off"]
        g.setdefault("dbg_index", []).append(("s_u0", "dbgb", off, 2048))
        p.prod("gpsimd", "st", lambda gp, off=off: gp.dma_start(
            out=g["dbgb"][:, off:off + 2048], in_=g["s_u"][0]), 16)
        g["dbgb_off"] = off + 2048
        p.flush()

    emit_scan(p, nc, g)

    if DBG:
        dbg_tap(p, g, g["scandbg"][:, 0:32], "scandbg")
        off = g["dbgb_off"]
        g.setdefault("dbg_index", []).append(("states", "dbgb", off, 4096))
        p.prod("gpsimd", "st", lambda gp, off=off: gp.dma_start(
            out=g["dbgb"][:, off:off + 4096].rearrange(
                "p (a b) -> p a b", a=DC),
            in_=g["s_states"][:, :, 0:512]), 16)
        g["dbgb_off"] = off + 4096
        p.flush()

    with (
        nc.sbuf_tensor("mx_st", [128, DC * 512], BF16) as stb,
        nc.sbuf_tensor("mx_wo", [128, DC * D], BF16) as wo,
        nc.sbuf_tensor("mx_wof", [128, 2 * D], F32) as wof,
    ):
        wok = wo[:].rearrange("p (k e) -> p k e", k=DC)
        load_weight_bf16(p, g, g["l0Wout_T"], wok, wof[:])
        p.wait("tensor", "dve")
        stv = stb[:].rearrange("p (k t) -> p k t", k=DC)
        ps_stack = contextlib.ExitStack()
        pds = psum_list(nc, ps_stack, "mx_po_", 4, [128, 512])
        for tq in range(4):
            if tq > 0:
                p.wait("sync", "dve")
            p.prod("sync", "ld", lambda s, tq=tq: s.dma_start(
                out=stv[:], in_=g["s_states"][:, :, 512 * tq:512 * (tq + 1)]), 16)
            p.wait("tensor", "ld")
            for pair in range(2):
                if tq > 0 or pair > 0:
                    p.wait("tensor", "dve")
                for ii in (2 * pair, 2 * pair + 1):
                    for n2 in range(2):
                        def omm(t, ii=ii, n2=n2):
                            for k in range(DC):
                                t.matmul(pds[2 * (ii % 2) + n2][:],
                                         stv[:, k, 128 * ii:128 * (ii + 1)],
                                         wok[:, k, 512 * n2:512 * (n2 + 1)],
                                         start=(k == 0), stop=False)
                            ins = t.matmul(
                                pds[2 * (ii % 2) + n2][:], g["ones_bf"][:],
                                g["brow_bf"][0:1, 2 * D + 512 * n2:
                                             2 * D + 512 * (n2 + 1)],
                                start=False, stop=True)
                            return ins
                        p.prod("tensor", "pe", omm)
                p.wait("vector", "pe")
                for ii in (2 * pair, 2 * pair + 1):
                    for n2 in range(2):
                        i = 4 * tq + ii
                        xs = x_res[:, i * D + 512 * n2:i * D + 512 * (n2 + 1)]
                        p.prod("vector", "dve", lambda v, xs=xs, ii=ii, n2=n2:
                               v.tensor_tensor(xs, pds[2 * (ii % 2) + n2][:], xs,
                                               op=OP.add))
        p.flush()
        ps_stack.close()


def emit_scan(p, nc, g):
    with (
        nc.sbuf_tensor("sc_at", [128, DC * D], BF16) as atb,
        nc.sbuf_tensor("sc_atf", [128, DC * D], F32) as atf,
        nc.sbuf_tensor("sc_i2", [128, 2 * DC * SC], F32) as i2b,
        nc.sbuf_tensor("sc_g", [128, 2 * DC * SC], F32) as gb,
        nc.sbuf_tensor("sc_u", [128, 2 * DC * SC], BF16) as ub,
        nc.sbuf_tensor("sc_stt", [128, 2 * DC * SC], BF16) as stt,
        nc.sbuf_tensor("sc_s0", [128, DC], BF16) as s0,
        nc.sbuf_tensor("sc_th", [128, DC], F32) as th,
        nc.sbuf_tensor("sc_thw", [128, DC], F32) as thw,
    ):
        p.prod("sync", "ld", lambda s: s.dma_start(
            out=atf[:].rearrange("p (k e) -> p k e", k=DC),
            in_=g["l0A_T"].rearrange("(k p) e -> p k e", p=128)), 16)
        p.wait("vector", "ld")
        p.prod("vector", "dve", lambda v: v.tensor_copy(atb[:], atf[:]))
        p.prod("vector", "dve", lambda v: v.memset(s0[:], 0.0))
        for nm, buf, view in (("s_i2", i2b, None), ("s_g", gb, None),
                              ("s_u", ub, None)):
            p.prod("sync", "ld", lambda s, nm=nm, buf=buf: s.dma_start(
                out=buf[:, 0:DC * SC], in_=g[nm][0]), 16)
        p.flush()

        atv = atb[:].rearrange("p (k e) -> p k e", k=DC)
        i2v = i2b[:].rearrange("p (h k t) -> p h k t", h=2, k=DC)
        gv = gb[:].rearrange("p (h k t) -> p h k t", h=2, k=DC)
        uv = ub[:].rearrange("p (h k t) -> p h k t", h=2, k=DC)
        stv = stt[:].rearrange("p (h k t) -> p h k t", h=2, k=DC)
        ps_stack = contextlib.ExitStack()
        psc = psum_list(nc, ps_stack, "sc_p", 8, [128, 1])

        ld0 = p.c["ld"]
        pe0 = p.c["pe"]
        act0 = p.c["act"]
        dve0 = p.c["dve"]
        st0 = p.c["st"]
        sems = p.sems

        def tbody(t):
            for tt in range(T):
                sc_i, ti = divmod(tt, SC)
                h = sc_i % 2
                if ti == 0 and sc_i > 0:
                    t.wait_ge(sems["ld"], ld0 + 48 * sc_i)
                if tt == 0:
                    for c in range(DC):
                        t.matmul(psc[c][:], atv[:, 0, 128 * c:128 * (c + 1)],
                                 s0[:, 0:1], start=True, stop=True
                                 ).then_inc(sems["pe"], 1)
                else:
                    hp = ((tt - 1) // SC) % 2
                    tp = (tt - 1) % SC
                    t.wait_ge(sems["dve"], dve0 + tt)
                    for c in range(DC):
                        for k in range(DC):
                            scol = hp * (DC * SC) + k * SC + tp
                            ins = t.matmul(psc[c][:],
                                           atv[:, k, 128 * c:128 * (c + 1)],
                                           stt[:, scol:scol + 1],
                                           start=(k == 0), stop=(k == DC - 1))
                        ins.then_inc(sems["pe"], 1)

        def sbody(s):
            for tt in range(T):
                h = (tt // SC) % 2
                ti = tt % SC
                for c in range(DC):
                    ucol = h * (DC * SC) + c * SC + ti
                    s.wait_ge(sems["pe"], pe0 + 8 * tt + c + 1)
                    s.activation(th[:, c:c + 1], psc[c][:], AF.Tanh,
                                 bias=ub[:, ucol:ucol + 1]
                                 ).then_inc(sems["act"], 1)

        def vbody(v):
            for tt in range(T):
                sc_i, ti = divmod(tt, SC)
                h = sc_i % 2
                if ti == 0 and sc_i >= 2:
                    v.wait_ge(sems["st"], st0 + 16 * (sc_i - 1))
                base = h * (DC * SC) + ti
                for c in range(DC):
                    v.wait_ge(sems["act"], act0 + 8 * tt + c + 1)
                    ins = v.tensor_scalar(
                        stt[:, base + c * SC:base + c * SC + 1],
                        th[:, c:c + 1], gv[:, h, c, ti:ti + 1],
                        i2v[:, h, c, ti:ti + 1], op0=OP.mult, op1=OP.add)
                    if c == DC - 1:
                        ins.then_inc(sems["dve"], 1)

        def syncbody(s):
            for sc_i in range(1, NSC):
                h = sc_i % 2
                s.wait_ge(sems["dve"], dve0 + SC * (sc_i - 1))
                for nm, buf in (("s_i2", i2v), ("s_g", gv), ("s_u", uv)):
                    s.dma_start(out=buf[:, h].rearrange("p k t -> p (k t)"),
                                in_=g[nm][sc_i]).then_inc(sems["ld"], 16)

        def gbody(gp):
            for sc_i in range(NSC):
                h = sc_i % 2
                gp.wait_ge(sems["dve"], dve0 + SC * (sc_i + 1))
                gp.dma_start(out=g["s_states"][:, :, SC * sc_i:SC * (sc_i + 1)],
                             in_=stv[:, h].rearrange("p k t -> p (k t)")
                             ).then_inc(sems["st"], 16)

        p.raw("tensor", tbody)
        p.raw("scalar", sbody)
        p.raw("vector", vbody)
        p.raw("sync", syncbody)
        p.raw("gpsimd", gbody)
        p.c["pe"] += 8 * T
        p.c["act"] += 8 * T
        p.c["dve"] += T
        p.c["ld"] += 48 * (NSC - 1)
        p.c["st"] += 16 * NSC
        p.wait("gpsimd", "st")
        p.flush()
        ps_stack.close()



def emit_dequant(p, nc, g):
    with (
        nc.sbuf_tensor("dq_w", [128, 2 * IF], F32) as wbuf,
        nc.sbuf_tensor("dq_i", [128, IF], I32) as ibuf,
        nc.sbuf_tensor("dq_d", [128, IF], BF16) as dbuf,
        nc.sbuf_tensor("dq_s", [128, 96 * 2], F32) as sbuf_s,
        nc.sbuf_tensor("dq_o", [128, IFC * 128], BF16) as obuf,
        nc.psum_tensor("dq_tb0", [128, 128], BF16) as tb0,
        nc.psum_tensor("dq_tb1", [128, 128], BF16) as tb1,
    ):
        g["trp_bf"] = (tb0, tb1)
        jobs = []
        alljobs = {"wu": (g["l1wu"], 24, D, g["s_wuT"], DC),
                   "wg": (g["l1wg"], 24, D, g["s_wgT"], DC),
                   "wd": (g["l1wd"], 8, IF, g["s_wdT"], IFC)}
        for nm in DEQ_JOBS:
            jobs.append(alljobs[nm])
        for (src, ntl, cols, dst, nch) in jobs:
            nb = cols // 32
            srcv = src.rearrange("(r p) c -> r p c", p=128)
            for r in range(ntl):
                sl = r % 2
                p.wait("sync", "dve")
                p.prod("sync", "ld", lambda s, r=r, sl=sl, srcv=srcv, cols=cols:
                       s.dma_start(out=wbuf[:, sl * IF:sl * IF + cols],
                                   in_=srcv[r]), 16)
                p.wait("vector", "ld")
                p.wait("vector", "pe")
                p.wait("vector", "st")
                w3 = wbuf[:, sl * IF:sl * IF + cols].rearrange(
                    "p (b j) -> p b j", j=32)
                i3 = ibuf[:, 0:cols].rearrange("p (b j) -> p b j", j=32)
                sb = sbuf_s[:, 0:nb]
                rsb = sbuf_s[:, 96:96 + nb]
                def dqc(v, w3=w3, i3=i3, sb=sb, rsb=rsb, cols=cols, nb=nb):
                    v.tensor_reduce(sb, w3, axis=AX, op=OP.max,
                                    apply_absolute_value=True)
                    v.drain()
                    v.tensor_scalar(sb, sb, 1.0 / 7.0, 1e-12, op0=OP.mult,
                                    op1=OP.max)
                    v.drain()
                    v.reciprocal(rsb, sb)
                    v.drain()
                    v.tensor_tensor(i3, w3, rsb.unsqueeze(-1).broadcast_to(
                        (128, nb, 32)), op=OP.mult)
                    v.drain()
                    return v.tensor_tensor(dbuf[:, 0:cols].rearrange(
                        "p (b j) -> p b j", j=32), i3,
                        sb.unsqueeze(-1).broadcast_to((128, nb, 32)),
                        op=OP.mult)
                p.prod("vector", "dve", dqc)
                items = []
                for c in range(cols // 128):
                    items.append((dbuf[:, 128 * c:128 * (c + 1)],
                                  obuf[:, 128 * c:128 * (c + 1)]))
                transpose_pass(p, g, items, BF16)
                p.wait("gpsimd", "dve")
                for c in range(cols // 128):
                    p.prod("gpsimd", "st", lambda gp, dst=dst, c=c, r=r:
                           gp.dma_start(out=dst[:, c, 128 * r:128 * (r + 1)],
                                        in_=obuf[:, 128 * c:128 * (c + 1)]), 16)
        p.wait("gpsimd", "st")
        if DBG:
            off = g["dbgb_off"]
            g.setdefault("dbg_index", []).append(("wuT0", "dbgb", off, 3072))
            p.prod("gpsimd", "st", lambda gp, off=off: gp.dma_start(
                out=g["dbgb"][:, off:off + 3072],
                in_=g["s_wuT"][:, 0, :]), 16)
            g["dbgb_off"] = off + 3072
        p.flush()



def emit_qdqffn(p, nc, g, tag):
    x_res = g["x_res"]
    scal = g["scal"]
    amax_sb = g["amax_sb"]

    def compute_scale(col_src, col_s, col_rs, qmax):
        def csc(v):
            v.drain()
            v.tensor_scalar(scal[:, col_s:col_s + 1],
                            amax_sb[1][:, col_src:col_src + 1], 1.0 / qmax,
                            1e-12, op0=OP.mult, op1=OP.max)
            v.drain()
            return v.reciprocal(scal[:, col_rs:col_rs + 1],
                                scal[:, col_s:col_s + 1])
        p.prod("vector", "dve", csc)

    with nc.sbuf_tensor(f"f_xq{tag}", [128, DC * T], BF16) as xq:
        xqv = xq[:].rearrange("p (k t) -> p k t", k=DC)
        with (
            nc.sbuf_tensor(f"f_xn{tag}", [128, NT * D], F32) as xnf,
            nc.sbuf_tensor(f"f_mx{tag}", [128, NT], F32) as mxcol,
            nc.sbuf_tensor(f"f_qi{tag}", [128, D], I32) as qi,
            nc.sbuf_tensor(f"f_qb{tag}", [128, D], BF16) as qb,
            nc.psum_tensor(f"f_tb0{tag}", [128, 128], BF16) as tb0,
            nc.psum_tensor(f"f_tb1{tag}", [128, 128], BF16) as tb1,
            nc.psum_tensor(f"f_arp1{tag}", [128, 128], F32) as arp1,
        ):
            g["trp_bf"] = (tb0, tb1)
            g["ar_psum"] = arp1
            ln_pass(p, g, x_res, list(range(NT)), xnf, 0)
            for i in range(NT):
                p.prod("vector", "dve", lambda v, i=i: v.tensor_reduce(
                    mxcol[:, i:i + 1], xnf[:, i * D:(i + 1) * D], axis=AX,
                    op=OP.max, apply_absolute_value=True))
            p.prod("vector", "dve", lambda v: (v.drain(), v.tensor_reduce(
                scal[:, 15:16], mxcol[:], axis=AX, op=OP.max), v.drain())[1])
            global_max_cols(p, nc, g, scal[:, 15:16], 1, amax_sb[1][:, 0:1])
            compute_scale(0, 0, 1, 32767.0)
            p.flush()
            if QDQ_MAX < 2:
                return

            for i in range(NT):
                p.wait("vector", "pe")
                p.prod("vector", "dve", lambda v, i=i: (
                    v.tensor_scalar(qi[:], xnf[:, i * D:(i + 1) * D],
                                    scal[:, 1:2], None, op0=OP.mult),
                    v.tensor_scalar(qb[:], qi[:], scal[:, 0:1], None,
                                    op0=OP.mult))[-1])
                items = []
                for c in range(DC):
                    items.append((qb[:, 128 * c:128 * (c + 1)],
                                  xqv[:, c, 128 * i:128 * (i + 1)]))
                transpose_pass(p, g, items, BF16)
            p.flush()
        if QDQ_MAX < 3:
            return

        with (
            nc.sbuf_tensor(f"f_wj{tag}", [128, 2 * 2 * DC * 512], BF16) as wj,
            nc.sbuf_tensor(f"f_stg{tag}", [128, 4 * 512], BF16) as stg,
            nc.sbuf_tensor(f"f_mx2{tag}", [128, 2 * 96], F32) as mx2,
            nc.psum_tensor(f"f_arp3{tag}", [128, 128], F32) as arp3,
        ):
            g["ar_psum"] = arp3
            wjv = wj[:].rearrange("p (s a k c) -> p s a k c", s=2, a=2, k=DC)
            ps_stack = contextlib.ExitStack()
            pup = psum_list(nc, ps_stack, f"f_pu{tag}", 4, [128, 512])
            dve_f3 = p.c["dve"]
            pe_j = {}
            for j in range(6):
                js = j % 2
                if j >= 2:
                    p.wait("sync", "pe", pe_j[j - 2])
                for a, nmw in ((0, "s_wuT"), (1, "s_wgT")):
                    p.prod("sync", "ld", lambda s, a=a, nmw=nmw, j=j, js=js:
                           s.dma_start(out=wjv[:, js, a],
                                       in_=g[nmw][:, :, 512 * j:512 * (j + 1)]),
                           16)
                p.wait("tensor", "ld")
                for i in range(NT):
                    it = NT * j + i
                    if it > 0:
                        p.wait("tensor", "act")
                        p.wait("tensor", "act")
                        p.wait("tensor", "dve")
                    for a in range(2):
                        def fmm(t, a=a, i=i, js=js):
                            for k in range(DC):
                                ins = t.matmul(
                                    pup[2 * (i % 2) + a][:],
                                    xqv[:, k, 128 * i:128 * (i + 1)],
                                    wjv[:, js, a, k], start=(k == 0),
                                    stop=(k == DC - 1))
                            return ins
                        p.prod("tensor", "pe", fmm)
                    p.wait("vector", "pe")
                    p.wait("scalar", "pe")
                    for a in range(2):
                        p.prod("vector", "dve", lambda v, a=a, i=i, it=it:
                               v.tensor_reduce(
                                   mx2[:, 96 * a + it % 96:
                                       96 * a + it % 96 + 1],
                                   pup[2 * (i % 2) + a][:], axis=AX,
                                   op=OP.max, apply_absolute_value=True))
                        p.wait("scalar", "st")
                        p.wait("scalar", "dve")
                        p.prod("scalar", "act", lambda s, a=a, i=i:
                               s.copy(stg[:, 512 * (2 * (i % 2) + a):
                                          512 * (2 * (i % 2) + a + 1)],
                                      pup[2 * (i % 2) + a][:]))
                        p.wait("gpsimd", "act")
                        nms = "s_up" if a == 0 else "s_gate"
                        p.prod("gpsimd", "st", lambda gp, a=a, i=i, j=j,
                               nms=nms: gp.dma_start(
                                   out=g[nms][i, :, 512 * j:512 * (j + 1)],
                                   in_=stg[:, 512 * (2 * (i % 2) + a):
                                           512 * (2 * (i % 2) + a + 1)]), 16)
                pe_j[j] = p.c["pe"]
            p.prod("vector", "dve", lambda v: (
                v.drain(),
                v.tensor_reduce(scal[:, 12:13], mx2[:, 0:96], axis=AX,
                                op=OP.max),
                v.tensor_reduce(scal[:, 13:14], mx2[:, 96:192], axis=AX,
                                op=OP.max), v.drain())[-2])
            global_max_cols(p, nc, g, scal[:, 12:14], 2, amax_sb[1][:, 0:2])
            compute_scale(0, 2, 3, 32767.0)
            compute_scale(1, 4, 5, 32767.0)
            p.flush()
            ps_stack.close()
    if QDQ_MAX < 5:
        return

    with (
        nc.sbuf_tensor(f"f_ta{tag}", [128, 2 * IF], BF16) as ta,
        nc.sbuf_tensor(f"f_tb{tag}", [128, IF], F32) as tb,
        nc.sbuf_tensor(f"f_tbb{tag}", [128, IF], BF16) as tbb,
        nc.sbuf_tensor(f"f_ti{tag}", [128, IF], I32) as ti,
        nc.sbuf_tensor(f"f_tc{tag}", [128, IF], F32) as tc,
        nc.sbuf_tensor(f"f_tcb{tag}", [128, IF], BF16) as tcb,
        nc.sbuf_tensor(f"f_mx3{tag}", [128, NT], F32) as mx3,
        nc.psum_tensor(f"f_arp5{tag}", [128, 128], F32) as arp5,
    ):
        g["ar_psum"] = arp5
        for i in range(NT):
            sl = i % 2
            if i >= 1:
                p.wait("sync", "st")
                p.wait("vector", "st")
            p.prod("sync", "ld", lambda s, i=i, sl=sl: s.dma_start(
                out=ta[:, sl * IF:(sl + 1) * IF], in_=g["s_gate"][i]), 16)
            p.wait("vector", "ld")
            gat = ta[:, sl * IF:(sl + 1) * IF]
            p.prod("vector", "dve", lambda v, gat=gat: (
                v.tensor_scalar(ti[:], gat, scal[:, 5:6], None, op0=OP.mult),
                v.tensor_scalar(tb[:], ti[:], scal[:, 4:5], None,
                                op0=OP.mult))[-1])
            p.wait("scalar", "dve")
            p.prod("scalar", "act",
                   lambda s: s.activation(tc[:], tb[:], AF.Sigmoid))
            p.wait("vector", "act")
            p.prod("vector", "dve", lambda v, i=i: (
                v.tensor_scalar(ti[:], tc[:], 65536.0, 65535.0, op0=OP.mult,
                                op1=OP.min),
                v.tensor_scalar(tc[:], ti[:], 1.0 / 65536.0, None,
                                op0=OP.mult),
                v.tensor_tensor(tcb[:], tb[:], tc[:], op=OP.mult),
                v.drain(),
                v.tensor_reduce(mx3[:, i:i + 1], tcb[:], axis=AX, op=OP.max,
                                apply_absolute_value=True))[-1])
            p.wait("gpsimd", "dve")
            p.prod("gpsimd", "st", lambda gp, i=i: gp.dma_start(
                out=g["s_g2"][i], in_=tcb[:]), 16)
        p.prod("vector", "dve", lambda v: (v.drain(), v.tensor_reduce(
            scal[:, 15:16], mx3[:], axis=AX, op=OP.max), v.drain())[1])
        global_max_cols(p, nc, g, scal[:, 15:16], 1, amax_sb[1][:, 0:1])
        compute_scale(0, 6, 7, 32767.0)
        p.flush()
        if QDQ_MAX < 7:
            return

        for i in range(NT):
            sl = i % 2
            if i >= 1:
                p.wait("sync", "st")
                p.wait("vector", "st")
            p.prod("sync", "ld", lambda s, i=i, sl=sl: s.dma_start(
                out=ta[:, sl * IF:(sl + 1) * IF], in_=g["s_g2"][i]), 16)
            p.prod("sync", "ld", lambda s, i=i: s.dma_start(
                out=tbb[:], in_=g["s_up"][i]), 16)
            p.wait("vector", "ld")
            g2t = ta[:, sl * IF:(sl + 1) * IF]
            p.prod("vector", "dve", lambda v, g2t=g2t, i=i: (
                v.tensor_scalar(ti[:], g2t, scal[:, 7:8], None, op0=OP.mult),
                v.tensor_scalar(tc[:], ti[:], scal[:, 6:7], None,
                                op0=OP.mult),
                v.tensor_scalar(ti[:], tbb[:], scal[:, 3:4], None,
                                op0=OP.mult),
                v.tensor_scalar(tb[:], ti[:], scal[:, 2:3], None,
                                op0=OP.mult),
                v.tensor_tensor(tcb[:], tc[:], tb[:], op=OP.mult),
                v.drain(),
                v.tensor_reduce(mx3[:, i:i + 1], tcb[:], axis=AX, op=OP.max,
                                apply_absolute_value=True))[-1])
            p.wait("gpsimd", "dve")
            p.prod("gpsimd", "st", lambda gp, i=i: gp.dma_start(
                out=g["s_o"][i], in_=tcb[:]), 16)
        p.prod("vector", "dve", lambda v: (v.drain(), v.tensor_reduce(
            scal[:, 15:16], mx3[:], axis=AX, op=OP.max), v.drain())[1])
        global_max_cols(p, nc, g, scal[:, 15:16], 1, amax_sb[1][:, 0:1])
        compute_scale(0, 8, 9, 32767.0)
        if DBG:
            dbg_tap(p, g, scal[:, 0:16], f"scal_{tag}")
            for nm in ("s_up", "s_gate", "s_g2", "s_o"):
                off = g["dbgf_off"]
                g.setdefault("dbg_index", []).append((nm + "0_" + tag, "dbgf",
                                                      off, 3072))
                p.prod("gpsimd", "st", lambda gp, nm=nm, off=off: gp.dma_start(
                    out=g["dbgf"][:, off:off + 3072], in_=g[nm][0]), 16)
                g["dbgf_off"] = off + 3072
        p.flush()
        if QDQ_MAX < 9:
            return

    with (
        nc.sbuf_tensor(f"f_ta2{tag}", [128, 2 * IF], BF16) as ta,
        nc.sbuf_tensor(f"f_ti2{tag}", [128, IF], I32) as ti,
        nc.sbuf_tensor(f"f_wd{tag}", [128, IFC * D], BF16) as wdt,
        nc.sbuf_tensor(f"f_ob{tag}", [128, IF], BF16) as ob,
        nc.sbuf_tensor(f"f_oT{tag}", [128, IFC * 128], BF16) as oT,
        nc.psum_tensor(f"f_tc0{tag}", [128, 128], BF16) as tc0,
        nc.psum_tensor(f"f_tc1{tag}", [128, 128], BF16) as tc1,
    ):
        g["trp_bf"] = (tc0, tc1)
        wdv = wdt[:].rearrange("p (k d) -> p k d", k=IFC)
        oTv = oT[:].rearrange("p (k t) -> p k t", k=IFC)
        p.prod("sync", "ld", lambda s: s.dma_start(
            out=wdv[:], in_=g["s_wdT"][:]), 16)
        ps_stack = contextlib.ExitStack()
        pdn = psum_list(nc, ps_stack, f"f_pd{tag}", 2, [128, 512])
        for i in range(NT):
            sl = i % 2
            if i >= 2:
                p.wait("sync", "dve")
            p.prod("sync", "ld", lambda s, i=i, sl=sl: s.dma_start(
                out=ta[:, sl * IF:(sl + 1) * IF], in_=g["s_o"][i]), 16)
            p.wait("vector", "ld")
            p.wait("vector", "pe")
            ot = ta[:, sl * IF:(sl + 1) * IF]
            p.prod("vector", "dve", lambda v, ot=ot: (
                v.tensor_scalar(ti[:], ot, scal[:, 9:10], None,
                                op0=OP.mult),
                v.tensor_scalar(ob[:], ti[:], scal[:, 8:9], None,
                                op0=OP.mult))[-1])
            items = []
            for c in range(IFC):
                items.append((ob[:, 128 * c:128 * (c + 1)],
                              oTv[:, c, :]))
            transpose_pass(p, g, items, BF16)
            p.wait("tensor", "dve")
            if i == 0:
                p.wait("tensor", "ld")
            for n2 in range(2):
                def dmm(t, n2=n2):
                    for k in range(IFC):
                        ins = t.matmul(pdn[n2][:], oTv[:, k, :],
                                       wdv[:, k, 512 * n2:512 * (n2 + 1)],
                                       start=(k == 0), stop=(k == IFC - 1))
                    return ins
                p.prod("tensor", "pe", dmm)
            p.wait("vector", "pe")
            for n2 in range(2):
                xs = x_res[:, i * D + 512 * n2:i * D + 512 * (n2 + 1)]
                p.prod("vector", "dve", lambda v, xs=xs, n2=n2:
                       v.scalar_tensor_tensor(xs, pdn[n2][:], 0.5, xs,
                                              op0=OP.mult, op1=OP.add))
            p.wait("tensor", "dve")
        p.flush()
        ps_stack.close()



def emit_attn(p, nc, g):
    x_res = g["x_res"]
    slopes = _slopes(NH)
    with (
        nc.sbuf_tensor("at_v", [128, NT * NH * (HD + 1)], BF16) as vb,
        nc.sbuf_tensor("at_xs", [128, DC * 512], BF16) as xspan,
    ):
        vv = vb[:].rearrange("p (i h c) -> p i h c", i=NT, h=NH)
        xsv = xspan[:].rearrange("p (k t) -> p k t", k=DC)

        with (
            nc.sbuf_tensor("at_xn", [128, 4 * D], BF16) as xn,
            nc.psum_tensor("at_tb0", [128, 128], BF16) as tb0,
            nc.psum_tensor("at_tb1", [128, 128], BF16) as tb1,
        ):
            g["trp_bf"] = (tb0, tb1)
            for q in range(4):
                tiles = [4 * q + i for i in range(4)]
                if q > 0:
                    p.wait("vector", "st")
                ln_pass(p, g, x_res, tiles, xn, 4 * q)
                items = []
                for ii in range(4):
                    for c in range(DC):
                        items.append(
                            (xn[:, ii * D + 128 * c:ii * D + 128 * c + 128],
                             xsv[:, c, 128 * ii:128 * (ii + 1)]))
                transpose_pass(p, g, items, BF16)
                p.wait("gpsimd", "dve")
                for c in range(DC):
                    p.prod("gpsimd", "st", lambda gp, q=q, c=c: gp.dma_start(
                        out=g["s_xa"][:, c, 512 * q:512 * (q + 1)],
                        in_=xsv[:, c, :]), 16)
            p.wait("gpsimd", "st")
            p.flush()

        def load_xspan(tq, src_name="s_xa"):
            for c in range(DC):
                p.prod("sync", "ld", lambda s, tq=tq, c=c, src_name=src_name:
                       s.dma_start(out=xsv[:, c, :],
                                   in_=g[src_name][:, c, 512 * tq:512 * (tq + 1)]),
                       16)
            p.wait("tensor", "ld")

        with (
            nc.sbuf_tensor("at_w", [128, DC * D], BF16) as wq,
            nc.sbuf_tensor("at_wf", [128, D], F32) as wqf,
            nc.sbuf_tensor("at_st", [128, 2 * 512], BF16) as qstg,
            nc.psum_tensor("at_p0", [128, 512], F32) as ap0,
            nc.psum_tensor("at_p1", [128, 512], F32) as ap1,
        ):
            aps = [ap0, ap1]
            wkk = wq[:].rearrange("p (k e) -> p k e", k=DC)

            for (wt, snm, bcol) in ((g["l1Wq_T"], "s_qa", 3),
                                    (g["l1Wk_T"], "s_ka", 4)):
                load_weight_bf16(p, g, wt, wkk, wqf[:])
                p.wait("tensor", "dve")
                act0 = p.c["act"]
                jj = 0
                for tq in range(4):
                    load_xspan(tq)
                    for m in range(DC):
                        if jj >= 2:
                            p.wait("tensor", "act", act0 + jj - 1)

                        def qmm(t, tq=tq, m=m, jj=jj):
                            for k in range(DC):
                                ins = t.matmul(aps[jj % 2][:],
                                               wkk[:, k, 128 * m:128 * (m + 1)],
                                               xsv[:, k, :],
                                               start=(k == 0), stop=(k == DC - 1))
                            return ins
                        p.prod("tensor", "pe", qmm)
                        p.wait("scalar", "pe")
                        p.wait("scalar", "st")
                        bia = g["bias_cm"][:, bcol * 8 + m:bcol * 8 + m + 1]
                        p.prod("scalar", "act",
                               lambda s, m=m, jj=jj, bia=bia:
                               s.activation(qstg[:, 512 * (jj % 2):
                                                 512 * (jj % 2) + 512],
                                            aps[jj % 2][:], AF.Identity,
                                            bias=bia))
                        p.wait("gpsimd", "act")
                        p.prod("gpsimd", "st",
                               lambda gp, snm=snm, tq=tq, m=m, jj=jj:
                               gp.dma_start(
                                   out=g[snm][:, m, 512 * tq:512 * (tq + 1)],
                                   in_=qstg[:, 512 * (jj % 2):
                                            512 * (jj % 2) + 512]), 16)
                        jj += 1
                    p.wait("sync", "act")
                p.flush()

            load_weight_bf16(p, g, g["l1Wv_T"], wkk, wqf[:])
            p.wait("tensor", "dve")
            p.prod("vector", "dve", lambda v: v.memset(
                vv[:, :, :, HD:HD + 1], 1.0))
            dve_v0 = p.c["dve"]
            jj = 0
            for tq in range(4):
                load_xspan(tq)
                for i2_ in range(4):
                    i = 4 * tq + i2_
                    for n2 in range(2):
                        if jj >= 2:
                            p.wait("tensor", "dve", dve_v0 + 1 + jj - 1)

                        def vmm(t, i2_=i2_, n2=n2, jj=jj):
                            for k in range(DC):
                                t.matmul(aps[jj % 2][:],
                                         xsv[:, k, 128 * i2_:128 * (i2_ + 1)],
                                         wkk[:, k, 512 * n2:512 * (n2 + 1)],
                                         start=(k == 0), stop=False)
                            ins = t.matmul(
                                aps[jj % 2][:], g["ones_bf"][:],
                                g["brow_bf"][0:1, 512 * n2:512 * (n2 + 1)],
                                start=False, stop=True)
                            return ins
                        p.prod("tensor", "pe", vmm)
                        p.wait("vector", "pe")
                        dst = vv[:, i, 8 * n2:8 * (n2 + 1), 0:HD]
                        p.prod("vector", "dve", lambda v, dst=dst, jj=jj:
                               v.tensor_copy(dst, aps[jj % 2][:].rearrange(
                                   "p (h c) -> p h c", h=8)))
                        jj += 1
                p.wait("sync", "dve")
            p.flush()

        with (
            nc.sbuf_tensor("at_qh", [64, 2 * T], BF16) as qh,
            nc.sbuf_tensor("at_kh", [64, 2 * T], BF16) as kh,
            nc.sbuf_tensor("at_mc", [128, T], BF16) as mc,
            nc.sbuf_tensor("at_mm", [128, 1024], BF16) as mmk,
            nc.sbuf_tensor("at_am", [128, 1024], F32) as amf,
            nc.sbuf_tensor("at_pb0", [128, 2 * 512], BF16) as pb0,
            nc.sbuf_tensor("at_pb1", [128, 4 * 512], BF16) as pb1,
            nc.sbuf_tensor("at_osb", [128, 4 * 65], F32) as o_sb,
            nc.sbuf_tensor("at_ob", [128, HD], BF16) as obuf,
            nc.sbuf_tensor("at_oe", [64, 128], BF16) as oev,
            nc.psum_tensor("at_s0p", [128, 512], F32) as sp0,
            nc.psum_tensor("at_s1p", [128, 512], F32) as sp1,
            nc.psum_tensor("at_tp", [64, 128], BF16) as tpp,
        ):
            sps = [sp0, sp1]
            pbv = [pb1[:, 512 * i:512 * (i + 1)] for i in range(4)]
            pb0v = [pb0[:, 0:512], pb0[:, 512:1024]]
            ps_stack = contextlib.ExitStack()
            avs = psum_list(nc, ps_stack, "at_av", 4, [128, 65])
            pe_head = {}
            for h in range(NH):
                ch, rh = divmod(h, 2)
                rh *= HD
                sl = slopes[h]
                hb = h % 2
                if h >= 2:
                    p.wait("sync", "pe", pe_head[h - 2])
                p.prod("sync", "ld", lambda s, hb=hb, rh=rh, ch=ch: s.dma_start(
                    out=qh[:, hb * T:(hb + 1) * T],
                    in_=g["s_qa"][rh:rh + HD, ch, :]), 16)
                p.prod("sync", "ld", lambda s, hb=hb, rh=rh, ch=ch: s.dma_start(
                    out=kh[:, hb * T:(hb + 1) * T],
                    in_=g["s_ka"][rh:rh + HD, ch, :]), 16)
                p.prod("scalar", "act", lambda s, sl=sl: s.activation(
                    mc[:], g["iota_r"][:], AF.Exp, scale=-sl))
                p.prod("vector", "dve", lambda v, sl=sl: v.tensor_scalar(
                    amf[:], g["iota_m"][:], 0.0, -sl, op0=OP.max, op1=OP.mult))
                p.wait("scalar", "dve")
                p.prod("scalar", "act",
                       lambda s: s.activation(amf[:], amf[:], AF.Exp))
                p.wait("vector", "act")
                p.prod("vector", "dve", lambda v: v.tensor_tensor(
                    mmk[:], amf[:], g["zmask"][:], op=OP.mult))
                p.wait("tensor", "dve")
                p.wait("tensor", "ld")
                jj = 0
                exp_act = {}
                alibi_dve = {}
                av_pe = {}
                avq = []
                oep_pend = None

                def emit_av():
                    ki1, dvev, jj1, qc1 = avq.pop(0)
                    p.wait("tensor", "dve", dvev)
                    for qi in range(max(ki1, 4 * qc1), 4 * qc1 + 4):
                        qoff = 128 * (qi - 4 * qc1)

                        def avm(t, qi=qi, qoff=qoff, ki1=ki1, h=h, jj1=jj1):
                            return t.matmul(
                                avs[qi % 4][:],
                                pbv[jj1 % 4][:, qoff:qoff + 128],
                                vv[:, ki1, h, 0:HD + 1],
                                start=(ki1 == 0), stop=(ki1 == qi))
                        p.prod("tensor", "pe", avm)
                    av_pe[jj1] = p.c["pe"]

                def emit_oep():
                    qc1, osb_dve = oep_pend
                    for qi in range(4 * qc1, 4 * qc1 + 4):
                        sl4 = qi % 4
                        p.wait("vector", "st")

                        def oepf(v, sl4=sl4):
                            v.reciprocal(g["ln_sa"][:, 0:1],
                                         o_sb[:, 65 * sl4 + HD:65 * sl4 + HD + 1])
                            v.drain()
                            return v.tensor_scalar(
                                obuf[:], o_sb[:, 65 * sl4:65 * sl4 + HD],
                                g["ln_sa"][:, 0:1], None, op0=OP.mult)
                        p.prod("vector", "dve", oepf)
                        p.wait("tensor", "dve")
                        p.prod("tensor", "pe", lambda t: t.transpose(
                            tpp[:], obuf[:], g["ident_bf"][:]))
                        p.wait("vector", "pe")
                        p.prod("vector", "dve", lambda v: v.tensor_copy(
                            oev[:], tpp[:]))
                        p.wait("gpsimd", "dve")
                        p.prod("gpsimd", "st", lambda gp, qi=qi, rh=rh, ch=ch:
                               gp.dma_start(
                                   out=g["s_oa"][rh:rh + HD, ch,
                                                 128 * qi:128 * (qi + 1)],
                                   in_=oev[:]), 16)

                for qc in range(4):
                    for ki in range(4 * qc + 4):
                        off = 512 * qc - 128 * ki
                        if off <= -512:
                            continue

                        def smm(t, ki=ki, qc=qc, jj=jj, hb=hb):
                            return t.matmul(
                                sps[jj % 2][:],
                                kh[:, hb * T + 128 * ki:hb * T + 128 * (ki + 1)],
                                qh[:, hb * T + 512 * qc:hb * T + 512 * (qc + 1)],
                                start=True, stop=True)
                        if jj >= 2:
                            p.wait("tensor", "act", exp_act[jj - 2])
                        p.prod("tensor", "pe", smm)
                        if len(avq) >= 2:
                            emit_av()
                        if oep_pend is not None and ki >= 1:
                            emit_oep()
                            oep_pend = None
                        smm_pe = p.c["pe"]
                        p.wait("scalar", "pe", smm_pe)
                        if jj >= 2:
                            p.wait("scalar", "dve", alibi_dve[jj - 2])
                        p.prod("scalar", "act", lambda s, jj=jj: s.activation(
                            pb0v[jj % 2][:], sps[jj % 2][:], AF.Exp, scale=0.125))
                        exp_act[jj] = p.c["act"]
                        p.wait("vector", "act", exp_act[jj])
                        if jj >= 4:
                            p.wait("vector", "pe", av_pe.get(jj - 4, 0))
                        if off >= 128:
                            msl = mc[:, off - 127:off - 127 + 512]
                        else:
                            msl = mmk[:, off + 384:off + 384 + 512]
                        p.prod("vector", "dve",
                               lambda v, msl=msl, jj=jj: v.tensor_tensor(
                                   pbv[jj % 4][:], pb0v[jj % 2][:], msl,
                                   op=OP.mult))
                        alibi_dve[jj] = p.c["dve"]
                        avq.append((ki, p.c["dve"], jj, qc))
                        jj += 1
                    while avq:
                        emit_av()
                    if oep_pend is not None:
                        emit_oep()
                        oep_pend = None
                    p.wait("vector", "pe")
                    for qi in range(4 * qc, 4 * qc + 4):
                        sl4 = qi % 4
                        p.prod("vector", "dve", lambda v, sl4=sl4: v.tensor_copy(
                            o_sb[:, 65 * sl4:65 * (sl4 + 1)], avs[sl4][:]))
                    p.wait("tensor", "dve")
                    oep_pend = (qc, p.c["dve"])
                if oep_pend is not None:
                    emit_oep()
                    oep_pend = None
                pe_head[h] = p.c["pe"]
            p.wait("gpsimd", "st")
            p.flush()
            ps_stack.close()

        with (
            nc.sbuf_tensor("at_w2", [128, DC * D], BF16) as wo2,
            nc.sbuf_tensor("at_w2f", [128, D], F32) as wo2f,
        ):
            wok = wo2[:].rearrange("p (k e) -> p k e", k=DC)
            load_weight_bf16(p, g, g["l1Wo_T"], wok, wo2f[:])
            p.wait("tensor", "dve")
            ps_stack = contextlib.ExitStack()
            pds = psum_list(nc, ps_stack, "at_po_", 2, [128, 512])
            dve_a4 = p.c["dve"]
            jj = 0
            for tq in range(4):
                load_xspan(tq, "s_oa")
                for i2_ in range(4):
                    i = 4 * tq + i2_
                    for n2 in range(2):
                        if jj >= 2:
                            p.wait("tensor", "dve", dve_a4 + jj - 1)

                        def wmm(t, i2_=i2_, n2=n2, jj=jj):
                            for k in range(DC):
                                t.matmul(pds[jj % 2][:],
                                         xsv[:, k, 128 * i2_:128 * (i2_ + 1)],
                                         wok[:, k, 512 * n2:512 * (n2 + 1)],
                                         start=(k == 0), stop=False)
                            ins = t.matmul(pds[jj % 2][:], g["ones_bf"][:],
                                           g["brow_bf"][0:1, D + 512 * n2:
                                                        D + 512 * (n2 + 1)],
                                           start=False, stop=True)
                            return ins
                        p.prod("tensor", "pe", wmm)
                        p.wait("vector", "pe")
                        xs = x_res[:, i * D + 512 * n2:i * D + 512 * (n2 + 1)]
                        p.prod("vector", "dve", lambda v, xs=xs, jj=jj:
                               v.tensor_tensor(xs, pds[jj % 2][:], xs, op=OP.add))
                        jj += 1
                p.wait("sync", "dve")
            p.flush()
            ps_stack.close()



def build_nc(parts=ALL_PARTS):
    nc = bass.Bass("TRN2", target_bir_lowering=False)
    g = {}
    x_in = nc.dram_tensor("x", [128, NT * D], F32, kind="ExternalInput").ap()
    out = nc.dram_tensor("out", [128, NT * D], F32, kind="ExternalOutput").ap()

    def din(name, shape):
        ap = nc.dram_tensor(name, shape, F32, kind="ExternalInput").ap()
        g[name] = ap
        return ap

    for nm, shp in [("l0A_T", [D, D]), ("l0B_T", [D, D]), ("l0Win_T", [D, D]),
                    ("l0Wg_T", [D, D]), ("l0Wout_T", [D, D]),
                    ("l0b_in", [D]), ("l0b_gate", [D]), ("l0b_out", [D]),
                    ("moe_rw", [D, NE]),
                    ("moe_wg_r", [NE, D, MID_R]), ("moe_wu_r", [NE, D, MID_R]),
                    ("moe_wd_r", [NE, MID_R, D]),
                    ("moe_wg_s", [NSH, D, MID_S]), ("moe_wu_s", [NSH, D, MID_S]),
                    ("moe_wd_s", [NSH, MID_S, D]),
                    ("l1Wq_T", [D, D]), ("l1Wk_T", [D, D]), ("l1Wv_T", [D, D]),
                    ("l1Wo_T", [D, D]),
                    ("l1bq", [D]), ("l1bk", [D]), ("l1bv", [D]), ("l1bo", [D]),
                    ("l1wg", [IF, D]), ("l1wu", [IF, D]), ("l1wd", [D, IF])]:
        din(nm, shp)

    def scr(name, shape, dt):
        ap = nc.dram_tensor(name, shape, dt).ap()
        g[name] = ap
        return ap

    scr("s_wgr", [NE, D, MID_R], BF16)
    scr("s_wur", [NE, D, MID_R], BF16)
    scr("s_wdr", [NE, MID_R, D], BF16)
    scr("s_wgs", [NSH, D, MID_S], BF16)
    scr("s_wus", [NSH, D, MID_S], BF16)
    scr("s_wds", [NSH, MID_S, D], BF16)
    scr("s_i2", [NSC, 128, DC * SC], F32)
    scr("s_g", [NSC, 128, DC * SC], F32)
    scr("s_u", [NSC, 128, DC * SC], BF16)
    scr("s_states", [128, DC, T], BF16)
    scr("s_up", [NT, 128, IF], BF16)
    scr("s_gate", [NT, 128, IF], BF16)
    scr("s_g2", [NT, 128, IF], BF16)
    scr("s_o", [NT, 128, IF], BF16)
    scr("s_xa", [128, DC, T], BF16)
    scr("s_qa", [128, DC, T], BF16)
    scr("s_ka", [128, DC, T], BF16)
    scr("s_oa", [128, DC, T], BF16)
    scr("s_wuT", [128, DC, IF], BF16)
    scr("s_wgT", [128, DC, IF], BF16)
    scr("s_wdT", [128, IFC, D], BF16)
    cc_in = scr("cc_in", [4, 1], F32)
    cc_out = scr("cc_out", [1, 4], F32)
    if DBG:
        g["dbgf"] = nc.dram_tensor("dbgf", [128, 32768], F32,
                                   kind="ExternalOutput").ap()
        g["dbgb"] = nc.dram_tensor("dbgb", [128, 32768], BF16,
                                   kind="ExternalOutput").ap()
        g["dbgf_off"] = 0
        g["dbgb_off"] = 0

    with contextlib.ExitStack() as stack:
        def sb(name, shape, dt):
            t = stack.enter_context(nc.sbuf_tensor(name, shape, dt))
            g[name] = t
            return t

        x_res = sb("x_res", [128, NT * D], F32)
        g["x_res"] = x_res[:]
        ident_bf = sb("ident_bf", [128, 128], BF16)
        ident_f32 = sb("ident_f32", [128, 128], F32)
        g["ident_bf"] = ident_bf
        g["ident_f32"] = ident_f32
        ones_bf = sb("ones_bf", [1, 128], BF16)
        g["ones_bf"] = ones_bf
        bias_cm = sb("bias_cm", [128, 8 * 8], F32)
        g["bias_cm"] = bias_cm
        brow_bf = sb("brow_bf", [1, 4 * D], BF16)
        g["brow_bf"] = brow_bf
        rw_f32 = sb("rw_f32", [128, DC * NE], F32)
        g["rw_f32"] = rw_f32
        g["ln_stats"] = sb("ln_stats", [128, 12], F32)
        g["ln_mv"] = sb("ln_mv", [128, 8], F32)
        g["ln_sa"] = sb("ln_sa", [128, 8], F32)
        g["scal"] = sb("scal", [128, 16], F32)
        g["iota_r"] = sb("iota_r", [128, T], F32)
        g["iota_m"] = sb("iota_m", [128, 1024], F32)
        g["zmask"] = sb("zmask", [128, 1024], F32)
        amax_a = sb("amax_a", [1, 8], F32)
        amax_b = sb("amax_b", [128, 4], F32)
        g["amax_sb"] = (amax_a, amax_b)
        g["cc_sb"] = sb("cc_sb", [1, 8], F32)
        g["cc_sb2"] = sb("cc_sb2", [4, 1], F32)
        g["ones_f32"] = sb("ones_f32", [1, 128], F32)
        if DBG:
            g["scandbg"] = sb("scandbg", [128, 32], F32)

        g["cc_in"] = cc_in
        g["cc_out"] = cc_out

        sems = {}
        for s in ("ld", "st", "pe", "dve", "act", "gps", "cc", "aux"):
            sems[s] = stack.enter_context(nc.semaphore(s))
        p = P(nc)
        p.sems = sems

        init_stack = contextlib.ExitStack()
        g["identsrc"] = init_stack.enter_context(
            nc.sbuf_tensor("identsrc", [128, 128], F32))
        g["zmask2"] = init_stack.enter_context(
            nc.sbuf_tensor("zmask2", [128, 128], F32))
        p.prod("sync", "ld", lambda s: s.dma_start(out=x_res[:], in_=x_in[:]), 16)
        for j, nm in enumerate(["l0b_in", "l0b_gate", "l0b_out", "l1bq", "l1bk",
                                "l1bv", "l1bo"]):
            p.prod("sync", "ld", lambda s, j=j, nm=nm: s.dma_start(
                out=bias_cm[:, 8 * j:8 * (j + 1)],
                in_=g[nm].rearrange("(c p) -> p c", p=128),
                allow_slow_non_contiguous=True), 16)
        p.prod("sync", "ld", lambda s: s.dma_start(
            out=rw_f32[:].rearrange("p (c n) -> p c n", c=DC),
            in_=g["moe_rw"].rearrange("(c p) n -> p c n", p=128)), 16)
        p.prod("gpsimd", "gps", lambda gp: (
            gp.iota(ident_f32[:], [[1, 128]], base=0, channel_multiplier=-1,
                    allow_small_or_imprecise_dtypes=True),
            gp.iota(g["iota_r"][:], [[1, T]], base=127, channel_multiplier=-1,
                    allow_small_or_imprecise_dtypes=True),
            gp.iota(g["iota_m"][:], [[1, 1024]], base=-384,
                    channel_multiplier=-1,
                    allow_small_or_imprecise_dtypes=True))[-1])
        p.wait("vector", "gps")
        p.wait("vector", "ld")
        p.prod("vector", "dve", lambda v: (
            v.tensor_scalar(g["zmask"][:], g["iota_m"][:], 0.0, None,
                            op0=OP.is_ge),
            v.tensor_scalar(g["identsrc"][:], ident_f32[:], 0.0, None,
                            op0=OP.is_ge),
            v.tensor_scalar(g["zmask2"][:], ident_f32[:], 0.5, None,
                            op0=OP.is_ge),
            v.drain(),
            v.tensor_tensor(g["identsrc"][:], g["identsrc"][:], g["zmask2"][:],
                            op=OP.subtract),
            v.drain(),
            v.tensor_copy(ident_f32[:], g["identsrc"][:]),
            v.tensor_copy(ident_bf[:], g["identsrc"][:]),
            v.memset(ones_bf[:], 1.0),
            v.memset(g["ones_f32"][:], 1.0))[-1])
        if DBG:
            dbg_tap(p, g, ident_f32[:], "identf")
            dbg_tap(p, g, ident_bf[:], "identb")
        p.flush()
        init_stack.close()

        with nc.sbuf_tensor("brow_f32", [1, 4 * D], F32) as brow_f32:
            for j, nm in enumerate(["l1bv", "l1bo", "l0b_out"]):
                p.prod("sync", "ld", lambda s, j=j, nm=nm: s.dma_start(
                    out=brow_f32[:, j * D:(j + 1) * D],
                    in_=g[nm].unsqueeze(0)), 16)
            p.wait("vector", "ld")
            p.prod("vector", "dve", lambda v: v.tensor_copy(
                brow_bf[:, 0:3 * D], brow_f32[:, 0:3 * D]))
            p.flush()

        if "l0f1" in parts or "l0f2" in parts:
            with (
                nc.sbuf_tensor("wc_a", [128, 2 * 4096], F32) as wc_a,
                nc.sbuf_tensor("wc_b", [128, 2 * 4096], BF16) as wc_b,
            ):
                jobs = []
                for snm, dnm in [("moe_wg_r", "s_wgr"), ("moe_wu_r", "s_wur"),
                                 ("moe_wd_r", "s_wdr"), ("moe_wg_s", "s_wgs"),
                                 ("moe_wu_s", "s_wus"), ("moe_wd_s", "s_wds")]:
                    src = g[snm].rearrange("a b c -> (a b c)")
                    dst = g[dnm].rearrange("a b c -> (a b c)")
                    tot = int(np.prod(g[snm].shape))
                    off = 0
                    while off < tot:
                        sz = min(128 * 4096, tot - off)
                        jobs.append((src, dst, off, sz // 4096))
                        off += sz
                dve_w = p.c["dve"]
                st_w = p.c["st"]
                for j, (src, dst, off, rows) in enumerate(jobs):
                    sl = j % 2
                    if j >= 2:
                        p.wait("sync", "dve", dve_w + j - 1)
                    p.prod("sync", "ld",
                           lambda s, src=src, off=off, rows=rows, sl=sl:
                           s.dma_start(out=wc_a[0:rows,
                                                sl * 4096:(sl + 1) * 4096],
                                       in_=src[off:off + rows * 4096].rearrange(
                                           "(p m) -> p m", m=4096)), 16)
                    p.wait("vector", "ld")
                    if j >= 2:
                        p.wait("vector", "st", st_w + 16 * (j - 1))
                    p.prod("vector", "dve",
                           lambda v, rows=rows, sl=sl: v.tensor_copy(
                               wc_b[0:rows, sl * 4096:(sl + 1) * 4096],
                               wc_a[0:rows, sl * 4096:(sl + 1) * 4096]))
                    p.wait("gpsimd", "dve")
                    p.prod("gpsimd", "st",
                           lambda gp, dst=dst, off=off, rows=rows, sl=sl:
                           gp.dma_start(out=dst[off:off + rows * 4096].rearrange(
                               "(p m) -> p m", m=4096),
                               in_=wc_b[0:rows, sl * 4096:(sl + 1) * 4096]), 16)
                p.wait("gpsimd", "st")
                p.flush()

        if "l0f1" in parts:
            emit_moe(p, nc, g, "a")
        if "l0mix" in parts:
            emit_mixer(p, nc, g)
        if "l0f2" in parts:
            emit_moe(p, nc, g, "b")
        if "l1f1" in parts or "l1f2" in parts or "deq" in parts:
            emit_dequant(p, nc, g)
        if "l1f1" in parts:
            emit_qdqffn(p, nc, g, "q1")
        if "l1att" in parts:
            emit_attn(p, nc, g)
        if "l1f2" in parts:
            emit_qdqffn(p, nc, g, "q2")

        p.prod("gpsimd", "st", lambda gp: gp.dma_start(out=out[:], in_=x_res[:]),
               16)
        p.wait("gpsimd", "st")
        p.flush()

    global LAST_DBG_INDEX
    LAST_DBG_INDEX = g.get("dbg_index", [])
    return nc



def _to_dev_x(xb):
    return np.ascontiguousarray(
        xb.reshape(NT, 128, D).transpose(1, 0, 2).reshape(128, NT * D))


def _from_dev_x(y):
    return np.ascontiguousarray(
        y.reshape(128, NT, D).transpose(1, 0, 2).reshape(T, D))


def make_in_maps(inputs, parts=ALL_PARTS):
    f = np.float32
    tr = lambda a: np.ascontiguousarray(np.asarray(a, f).T)
    cp = lambda a: np.ascontiguousarray(np.asarray(a, f))
    base = {
        "l0A_T": tr(inputs["L0_attn_A"]),
        "l0B_T": tr(inputs["L0_attn_B"]),
        "l0Win_T": tr(inputs["L0_attn_W_in"]),
        "l0Wg_T": tr(inputs["L0_attn_W_gate"]),
        "l0Wout_T": tr(inputs["L0_attn_W_out"]),
        "l0b_in": cp(inputs["L0_attn_b_in"]),
        "l0b_gate": cp(inputs["L0_attn_b_gate"]),
        "l0b_out": cp(inputs["L0_attn_b_out"]),
        "moe_rw": cp(inputs["L0_ffn_router_w"]),
        "moe_wg_r": cp(inputs["L0_ffn_wg_r"]),
        "moe_wu_r": cp(inputs["L0_ffn_wu_r"]),
        "moe_wd_r": cp(inputs["L0_ffn_wd_r"]),
        "moe_wg_s": cp(inputs["L0_ffn_wg_s"]),
        "moe_wu_s": cp(inputs["L0_ffn_wu_s"]),
        "moe_wd_s": cp(inputs["L0_ffn_wd_s"]),
        "l1Wq_T": tr(inputs["L1_attn_Wq"]),
        "l1Wk_T": tr(inputs["L1_attn_Wk"]),
        "l1Wv_T": tr(inputs["L1_attn_Wv"]),
        "l1Wo_T": tr(inputs["L1_attn_Wo"]),
        "l1bq": cp(inputs["L1_attn_bq"]),
        "l1bk": cp(inputs["L1_attn_bk"]),
        "l1bv": cp(inputs["L1_attn_bv"]),
        "l1bo": cp(inputs["L1_attn_bo"]),
        "l1wg": cp(inputs["L1_ffn_w_gate"]),
        "l1wu": cp(inputs["L1_ffn_w_up"]),
        "l1wd": cp(inputs["L1_ffn_w_down"]),
    }
    x = np.asarray(inputs["L0_attn_x"], f)
    return [dict(base, x=_to_dev_x(x[b])) for b in range(8)]


def kernel(**inputs):
    nc = build_nc(ALL_PARTS)
    in_maps = make_in_maps(inputs)
    res = run_bass_kernel_spmd(nc, in_maps, core_ids=list(range(8)))
    outs = [_from_dev_x(np.asarray(res.results[b]["out"])) for b in range(8)]
    return np.stack(outs).astype(np.float32)
